# version 10 (final)
# speedup vs baseline: 1.3200x; 1.0040x over previous
import os
import numpy as np
from contextlib import ExitStack

import concourse.bass as bass
import concourse.mybir as mybir
from concourse.ap import AP
from concourse.bass_utils import run_bass_kernel_spmd

F32 = mybir.dt.float32
BF16 = mybir.dt.bfloat16
AF = mybir.ActivationFunctionType
ALU = mybir.AluOpType

DM = 1024
H = 2048
N8 = 8
QQ = 32
LH = 4096
LT = LH + 3
CC = LH // QQ
NCORE = 8
EXP = 16
INTER = 512
TWO_PI = 6.283185307179586
MAGIC = 12582912.0


def build_nc(dbg_names=(), dbg_arena=()):
    nc = bass.Bass("TRN2", target_bir_lowering=False, debug=False,
                   detect_race_conditions=False)

    inp = {}
    def ein(name, shape):
        inp[name] = nc.dram_tensor(name, shape, F32, kind="ExternalInput")
        return inp[name]

    x_in = ein("x", [LT, DM])
    w_in = ein("W_in", [2 * H, DM])
    conv_w = ein("conv_w", [H, 4])
    conv_b = ein("conv_b", [H])
    log_dt = ein("log_dt", [H])
    log_A = ein("log_A_real", [H, N8])
    A_im = ein("A_imag", [H, N8])
    C_re = ein("C_re", [H, N8])
    C_im = ein("C_im", [H, N8])
    D_in = ein("D", [H])
    w_out = ein("W_out", [DM, H])
    gate_w = ein("gate_w", [EXP, DM])
    gate_b = ein("gate_b", [EXP])
    we1 = ein("We1", [EXP, DM, INTER])
    be1 = ein("be1", [EXP, INTER])
    we2 = ein("We2", [EXP, INTER, DM])
    be2 = ein("be2", [EXP, DM])
    we3 = ein("We3", [EXP, DM, INTER])
    be3 = ein("be3", [EXP, INTER])
    ws1 = ein("ws1", [DM, DM])
    bs1 = ein("bs1", [DM])
    ws2 = ein("ws2", [DM, DM])
    bs2 = ein("bs2", [DM])
    ws3 = ein("ws3", [DM, DM])
    bs3 = ein("bs3", [DM])
    sel = ein("sel", [1, NCORE])

    out_t = nc.dram_tensor("out", [LH, DM], F32, kind="ExternalOutput")

    u_h = nc.dram_tensor("u_h", [H, LH], BF16, kind="Internal")
    zs_h = nc.dram_tensor("zs_h", [H, LH], BF16, kind="Internal")
    kpT_h = nc.dram_tensor("kpT_h", [64, H], BF16, kind="Internal")
    ipv_h = nc.dram_tensor("ipv_h", [H, 2, QQ, N8], BF16, kind="Internal")
    opv_h = nc.dram_tensor("opv_h", [H, 2, N8, QQ], BF16, kind="Internal")
    muT_h = nc.dram_tensor("muT_h", [2, N8, H], F32, kind="Internal")
    x_fm_h = nc.dram_tensor("x_fm_h", [8, 128, LT], BF16, kind="Internal")
    dbg_out = {}

    st = ExitStack()
    def sb(name, shape, dt=BF16):
        return st.enter_context(nc.sbuf_tensor(name, shape, dt))

    idn = sb("idn", [128, 128], F32)
    idnb = sb("idnb", [128, 128], BF16)
    tri = sb("tri", [128, 128], BF16)
    onesb = sb("onesb", [128, 128], BF16)
    ramp = sb("ramp", [128, 128], F32)
    p_logdt = sb("p_logdt", [128, 16], F32)
    p_logA = sb("p_logA", [128, 128], F32)
    p_Aim = sb("p_Aim", [128, 128], F32)
    p_Cre = sb("p_Cre", [128, 128], F32)
    p_Cim = sb("p_Cim", [128, 128], F32)
    p_D = sb("p_D", [128, 16], F32)
    p_cw = sb("p_cw", [128, 64], F32)
    p_cb = sb("p_cb", [128, 16], F32)
    d_re = sb("d_re", [128, 128], F32)
    d_im = sb("d_im", [128, 128], F32)
    d_Ctr = sb("d_Ctr", [128, 128], F32)
    d_Cti = sb("d_Cti", [128, 128], F32)
    mu_re = sb("mu_re", [128, 128], F32)
    mu_im = sb("mu_im", [128, 128], F32)
    tmp1 = sb("tmp1", [128, 512], F32)
    tmp2 = sb("tmp2", [128, 512], F32)
    tmp3 = sb("tmp3", [128, 512], F32)
    tmp4 = sb("tmp4", [128, 512], F32)
    rrx = sb("rrx", [128, 512], F32)
    kk_built = sb("kk_built", [128, 512], F32)

    AR = 88000
    arena = sb("arena", [128, AR], BF16)

    def rb(off, n, p0=0, pn=128):
        return arena[p0:p0 + pn, off:off + n]
    def rf(off_b16, n_f32, p0=0, pn=128):
        return arena[p0:p0 + pn, off_b16:off_b16 + 2 * n_f32].bitcast(F32)

    O_XFM = 0
    O_WINT = 32800
    O_STAGE = 65600
    O_ZTILE = 65600
    O_CONV = 69700
    O_GEN = 0
    O_GENB = 16384
    O_KPT = 24600
    O_KPTB = 28700
    O_MUT = 30800

    pb = [st.enter_context(nc.psum_tensor(f"pb{i}", [128, 512], F32)) for i in range(6)]
    pbt = [st.enter_context(nc.psum_tensor(f"pbt{i}", [128, 1024], BF16)) for i in range(2)]

    bar = st.enter_context(nc.semaphore("bar"))
    dsem_s = st.enter_context(nc.semaphore("dsem_s"))
    dsem_g = st.enter_context(nc.semaphore("dsem_g"))
    dcnt = {"s": 0, "g": 0}

    def dma(eng, q, out, in_):
        ins = eng.dma_start(out=out, in_=in_)
        dcnt[q] += 16
        ins.then_inc(dsem_s if q == "s" else dsem_g, 16)

    def dwait(eng, q):
        eng.wait_ge(dsem_s if q == "s" else dsem_g, dcnt[q])

    stages = []
    def stage(name, **fns):
        stages.append((name, fns))

    def rr_args(eng, ph, sin_arg, cos_arg, scratch):
        eng.tensor_scalar(scratch, ph, 1.0 / TWO_PI, MAGIC, ALU.mult, ALU.add)
        eng.tensor_scalar(scratch, scratch, -MAGIC, TWO_PI, ALU.add, ALU.mult)
        eng.tensor_sub(sin_arg, ph, scratch)
        eng.tensor_scalar_add(cos_arg, ph, TWO_PI / 4.0)
        eng.tensor_scalar(scratch, cos_arg, 1.0 / TWO_PI, MAGIC, ALU.mult, ALU.add)
        eng.tensor_scalar(scratch, scratch, -MAGIC, TWO_PI, ALU.add, ALU.mult)
        return eng.tensor_sub(cos_arg, cos_arg, scratch)

    def p0_sync(eng):
        def mat_fm(dst, src_t, k):
            for t in range(16):
                src = AP(tensor=src_t.ap().tensor, offset=t * 128 * k,
                         ap=[[k, 128], [1, k]])
                dma(eng, "s", dst[:, t * k:(t + 1) * k], src)
        mat_fm(p_logdt, log_dt, 1)
        mat_fm(p_D, D_in, 1)
        mat_fm(p_cb, conv_b, 1)
        mat_fm(p_logA, log_A, N8)
        mat_fm(p_Aim, A_im, N8)
        mat_fm(p_Cre, C_re, N8)
        mat_fm(p_Cim, C_im, N8)
        mat_fm(p_cw, conv_w, 4)
        dwait(eng, "s")
    stage("p0_load", sync=p0_sync)

    ramp_h = nc.dram_tensor("ramp_h", [128], F32, kind="Internal")
    rampP = sb("rampP", [128, 8], F32)
    ramp_bf = sb("ramp_bf", [1, 128], BF16)

    def p1_a(eng):
        eng.memset(onesb[:], 1.0)
        eng.memset(epsv[:], 1e-5)
        eng.memset(tmp1[:, 0:128], 1.0)
        return eng.memset(tmp2[:, 0:128], 0.0)
    stage("p1_a", vector=p1_a)

    def p1_v(eng):
        eng.tensor_tensor_scan(tmp3[0:1, 0:128], tmp1[0:1, 0:128], tmp2[0:1, 0:128],
                               -1.0, ALU.add, ALU.add)
        eng.tensor_copy(rrx[:, 0:128], tmp1[:, 0:128])
        return eng.tensor_copy(ramp_bf[0:1, :], tmp3[0:1, 0:128])
    stage("p1_v", vector=p1_v)

    def p1_dma(eng):
        dma(eng, "s", ramp_h.ap(), tmp3[0:1, 0:128])
        dwait(eng, "s")
        src_ = AP(tensor=ramp_h.ap().tensor, offset=0, ap=[[1, 128], [1, 1]])
        dma(eng, "s", rampP[:, 0:1], src_)
        dwait(eng, "s")
    stage("p1_dma", sync=p1_dma)

    def p2_mm(eng):
        return eng.matmul(pb[0][:, 0:128], onesb[0:1, :], ramp_bf[0:1, :],
                          start=True, stop=True)
    stage("p2_mm", tensor=p2_mm)

    def p3_v(eng):
        eng.tensor_copy(ramp[:], pb[0][:, 0:128])
        rp = rampP[:, 0:1].broadcast_to((128, 128))
        eng.tensor_tensor(idn[:], ramp[:], rp, ALU.is_equal)
        eng.tensor_tensor(tmp1[:, 0:128], ramp[:], rp, ALU.is_ge)
        eng.tensor_copy(tri[:], tmp1[:, 0:128])
        return eng.tensor_copy(idnb[:], idn[:])
    stage("p3_v", vector=p3_v)

    def p4_scalar(eng):
        eng.activation(tmp1[:, 0:16], p_logdt[:, 0:16], AF.Exp)
        return eng.activation(tmp2[:, 0:128], p_logA[:, 0:128], AF.Exp)
    stage("p4_exp", scalar=p4_scalar)

    def p5_vector(eng):
        dt_b = tmp1[:, 0:16].unsqueeze(2).broadcast_to((128, 16, N8))
        eng.tensor_tensor(d_re[:, 0:128].rearrange("p (t n) -> p t n", n=N8),
                          dt_b, tmp2[:, 0:128].rearrange("p (t n) -> p t n", n=N8),
                          ALU.mult)
        eng.tensor_scalar_mul(d_re[:, 0:128], d_re[:, 0:128], -1.0)
        return eng.tensor_tensor(d_im[:, 0:128].rearrange("p (t n) -> p t n", n=N8),
                                 dt_b, p_Aim[:, 0:128].rearrange("p (t n) -> p t n", n=N8),
                                 ALU.mult)
    stage("p5_reim", vector=p5_vector)

    def p6_scalar(eng):
        return eng.activation(tmp1[:, 0:128], d_re[:, 0:128], AF.Exp)
    stage("p6_mag", scalar=p6_scalar)

    def p7_vector(eng):
        return rr_args(eng, d_im[:, 0:128], tmp3[:, 0:128], tmp4[:, 0:128],
                       tmp2[:, 0:128])
    stage("p7_rr", vector=p7_vector)

    def p8_scalar(eng):
        eng.activation(tmp3[:, 0:128], tmp3[:, 0:128], AF.Sin)
        return eng.activation(tmp4[:, 0:128], tmp4[:, 0:128], AF.Sin)
    stage("p8_sincos", scalar=p8_scalar)

    def p9_vector(eng):
        eng.tensor_mul(tmp3[:, 0:128], tmp3[:, 0:128], tmp1[:, 0:128])
        return eng.tensor_mul(tmp4[:, 0:128], tmp4[:, 0:128], tmp1[:, 0:128])
    stage("p9_lam", vector=p9_vector)

    def p10_scalar(eng):
        return eng.activation(tmp2[:, 0:128], p_logA[:, 0:128], AF.Exp)
    stage("p10_expA", scalar=p10_scalar)

    def p11_vector(eng):
        ar = rrx[:, 0:128]
        eng.tensor_scalar_mul(ar, tmp2[:, 0:128], -1.0)
        ai = p_Aim[:, 0:128]
        den = tmp2[:, 0:128]
        eng.tensor_mul(den, ar, ar)
        den2 = rrx[:, 128:256]
        eng.tensor_mul(den2, ai, ai)
        eng.tensor_add(den, den, den2)
        return eng.reciprocal(rrx[:, 128:256], den)
    stage("p11_den", vector=p11_vector)

    def p12_vector(eng):
        ar = rrx[:, 0:128]
        inv = rrx[:, 128:256]
        ai = p_Aim[:, 0:128]
        numr = rrx[:, 256:384]
        eng.tensor_scalar_add(numr, tmp4[:, 0:128], -1.0)
        numi = tmp3[:, 0:128]
        qr = rrx[:, 384:512]
        eng.tensor_mul(qr, numr, ar)
        t = tmp1[:, 0:128]
        eng.tensor_mul(t, numi, ai)
        eng.tensor_add(qr, qr, t)
        eng.tensor_mul(qr, qr, inv)
        qi = tmp1[:, 0:128]
        eng.tensor_mul(qi, numi, ar)
        t2 = tmp1[:, 128:256]
        eng.tensor_mul(t2, numr, ai)
        eng.tensor_sub(qi, qi, t2)
        eng.tensor_mul(qi, qi, inv)
        eng.tensor_mul(d_Ctr[:, 0:128], p_Cre[:, 0:128], qr)
        eng.tensor_mul(t2, p_Cim[:, 0:128], qi)
        eng.tensor_sub(d_Ctr[:, 0:128], d_Ctr[:, 0:128], t2)
        eng.tensor_mul(d_Cti[:, 0:128], p_Cre[:, 0:128], qi)
        eng.tensor_mul(t2, p_Cim[:, 0:128], qr)
        return eng.tensor_add(d_Cti[:, 0:128], d_Cti[:, 0:128], t2)
    stage("p12_Ct", vector=p12_vector)

    def ramp_flat(width):
        return ramp[:, 0:width].unsqueeze(1).broadcast_to((128, 16, width))

    def mk_pow_stages(tag, exp_fn, emit_fn):
        for n in range(N8):
            def v_pre(eng, n=n):
                re_n = d_re[:, 0:128].rearrange("p (t n) -> p t n", n=N8)[:, :, n]
                im_n = d_im[:, 0:128].rearrange("p (t n) -> p t n", n=N8)[:, :, n]
                re_b = re_n.unsqueeze(2).broadcast_to((128, 16, QQ))
                im_b = im_n.unsqueeze(2).broadcast_to((128, 16, QQ))
                dd = tmp3[:, 0:512]
                exp_fn(eng, dd)
                t1 = tmp1[:, 0:512].rearrange("p (t s) -> p t s", s=QQ)
                ph = tmp2[:, 0:512]
                eng.tensor_tensor(t1, dd.rearrange("p (t s) -> p t s", s=QQ),
                                  re_b, ALU.mult)
                eng.tensor_tensor(ph.rearrange("p (t s) -> p t s", s=QQ),
                                  dd.rearrange("p (t s) -> p t s", s=QQ),
                                  im_b, ALU.mult)
                return rr_args(eng, ph, tmp3[:, 0:512], tmp4[:, 0:512], rrx[:, 0:512])
            stage(f"{tag}{n}_a", vector=v_pre)

            def c_act(eng, n=n):
                eng.activation(tmp1[:, 0:512], tmp1[:, 0:512], AF.Exp)
                eng.activation(tmp3[:, 0:512], tmp3[:, 0:512], AF.Sin)
                return eng.activation(tmp4[:, 0:512], tmp4[:, 0:512], AF.Sin)
            stage(f"{tag}{n}_b", scalar=c_act)

            def v_post(eng, n=n):
                eng.tensor_mul(tmp4[:, 0:512], tmp4[:, 0:512], tmp1[:, 0:512])
                eng.tensor_mul(tmp3[:, 0:512], tmp3[:, 0:512], tmp1[:, 0:512])
                return emit_fn(eng, n)
            stage(f"{tag}{n}_c", vector=v_post)

    def d_rev(eng, dd):
        eng.tensor_scalar(dd.rearrange("p (t s) -> p t s", s=QQ), ramp_flat(QQ),
                          -1.0, 31.0, ALU.mult, ALU.add)
    def d_jp1(eng, dd):
        eng.tensor_scalar(dd.rearrange("p (t s) -> p t s", s=QQ), ramp_flat(QQ),
                          1.0, 1.0, ALU.mult, ALU.add)

    def nb(x, n):
        col = x[:, 0:128].rearrange("p (t n) -> p t n", n=N8)[:, :, n]
        return col.unsqueeze(2).broadcast_to((128, 16, QQ))

    def kk_emit(eng, n):
        term = tmp2[:, 0:512]
        t4 = tmp4[:, 0:512].rearrange("p (t s) -> p t s", s=QQ)
        t3 = tmp3[:, 0:512].rearrange("p (t s) -> p t s", s=QQ)
        eng.tensor_tensor(term.rearrange("p (t s) -> p t s", s=QQ), t4,
                          nb(d_Ctr, n), ALU.mult)
        tt = rrx[:, 0:512]
        eng.tensor_tensor(tt.rearrange("p (t s) -> p t s", s=QQ), t3,
                          nb(d_Cti, n), ALU.mult)
        eng.tensor_sub(term, term, tt)
        eng.tensor_scalar_mul(term, term, 2.0)
        if n == 0:
            return eng.tensor_copy(kk_built[:], term)
        return eng.tensor_add(kk_built[:], kk_built[:], term)
    mk_pow_stages("kk", d_rev, kk_emit)

    def kkD(eng):
        v = kk_built[:].rearrange("p (t s) -> p t s", s=QQ)[:, :, 31:32]
        return eng.tensor_tensor(v, v, p_D[:, 0:16].unsqueeze(2), ALU.add)
    stage("kkD", vector=kkD)

    kpt_f = lambda: rf(O_KPT, 2048, 0, 64)
    kpt_b = lambda: rb(O_KPTB, 2048, 0, 64)

    def kpt_zero(eng):
        return eng.memset(kpt_f(), 0.0)
    stage("kpt_zero", gpsimd=kpt_zero)
    for grp4 in range(4):
        def tmm(eng, g=grp4):
            last = None
            for i in range(4):
                t = g * 4 + i
                last = eng.transpose(pb[i][0:32, 0:128],
                                     kk_built[:, t * 32:(t + 1) * 32], idn[:])
            return last
        stage(f"kpt_t{grp4}", tensor=tmm)
        def tev(eng, g=grp4):
            last = None
            for i in range(4):
                t = g * 4 + i
                last = eng.tensor_copy(rf(O_KPT, 2048, 0, 64)[0:32, t * 128:(t + 1) * 128],
                                       pb[i][0:32, 0:128])
            return last
        stage(f"kpt_e{grp4}", vector=tev)
    def kpt_cast(eng):
        return eng.tensor_copy(kpt_b(), kpt_f())
    stage("kpt_cast", vector=kpt_cast)
    def kpt_store(eng):
        dma(eng, "s", kpT_h.ap(), kpt_b())
        dwait(eng, "s")
    stage("kpt_store", sync=kpt_store)

    gen_f = lambda: rf(O_GEN, 8192)
    gen_b = lambda: rb(O_GENB, 8192 * 2)[:, 0:8192]

    def ip_emit(eng, n):
        g4 = gen_f().rearrange("p (t r m n) -> p t r m n", r=2, m=QQ, n=N8)
        eng.tensor_copy(g4[:, :, 0, :, n], tmp4[:, 0:512].rearrange("p (t m) -> p t m", m=QQ))
        return eng.tensor_copy(g4[:, :, 1, :, n], tmp3[:, 0:512].rearrange("p (t m) -> p t m", m=QQ))
    mk_pow_stages("ip", d_rev, ip_emit)

    def ip_cast(eng):
        return eng.tensor_copy(gen_b(), gen_f())
    stage("ip_cast", vector=ip_cast)
    def ip_store(eng):
        dst = AP(tensor=ipv_h.ap().tensor, offset=0,
                 ap=[[512, 128], [128 * 512, 16], [1, 512]])
        dma(eng, "s", dst, gen_b().rearrange("p (t f) -> p t f", f=512))
        dwait(eng, "s")
    stage("ip_store", sync=ip_store)

    def op_emit(eng, n):
        vr = tmp2[:, 0:512]
        t4 = tmp4[:, 0:512].rearrange("p (t j) -> p t j", j=QQ)
        t3 = tmp3[:, 0:512].rearrange("p (t j) -> p t j", j=QQ)
        eng.tensor_tensor(vr.rearrange("p (t j) -> p t j", j=QQ), t4, nb(d_Ctr, n), ALU.mult)
        tt = rrx[:, 0:512]
        eng.tensor_tensor(tt.rearrange("p (t j) -> p t j", j=QQ), t3, nb(d_Cti, n), ALU.mult)
        eng.tensor_sub(vr, vr, tt)
        eng.tensor_scalar_mul(vr, vr, 2.0)
        vi = rrx[:, 0:512]
        eng.tensor_tensor(vi.rearrange("p (t j) -> p t j", j=QQ), t3, nb(d_Ctr, n), ALU.mult)
        tt2 = tmp1[:, 0:512]
        eng.tensor_tensor(tt2.rearrange("p (t j) -> p t j", j=QQ), t4, nb(d_Cti, n), ALU.mult)
        eng.tensor_add(vi, vi, tt2)
        eng.tensor_scalar_mul(vi, vi, -2.0)
        g4 = gen_f().rearrange("p (t r n j) -> p t r n j", r=2, n=N8, j=QQ)
        eng.tensor_copy(g4[:, :, 0, n, :], vr.rearrange("p (t j) -> p t j", j=QQ))
        return eng.tensor_copy(g4[:, :, 1, n, :], vi.rearrange("p (t j) -> p t j", j=QQ))
    mk_pow_stages("op", d_jp1, op_emit)

    def op_cast(eng):
        return eng.tensor_copy(gen_b(), gen_f())
    stage("op_cast", vector=op_cast)
    def op_store(eng):
        dst = AP(tensor=opv_h.ap().tensor, offset=0,
                 ap=[[512, 128], [128 * 512, 16], [1, 512]])
        dma(eng, "s", dst, gen_b().rearrange("p (t f) -> p t f", f=512))
        dwait(eng, "s")
    stage("op_store", sync=op_store)

    def mu_v1(eng):
        eng.tensor_scalar_mul(tmp1[:, 0:128], d_re[:, 0:128], float(QQ))
        ph = tmp2[:, 0:128]
        eng.tensor_scalar_mul(ph, d_im[:, 0:128], float(QQ))
        return rr_args(eng, ph, tmp3[:, 0:128], tmp4[:, 0:128], rrx[:, 0:128])
    stage("mu_v1", vector=mu_v1)

    def mu_s1(eng):
        eng.activation(tmp1[:, 0:128], tmp1[:, 0:128], AF.Exp)
        eng.activation(tmp3[:, 0:128], tmp3[:, 0:128], AF.Sin)
        return eng.activation(tmp4[:, 0:128], tmp4[:, 0:128], AF.Sin)
    stage("mu_s1", scalar=mu_s1)

    def mu_v2(eng):
        eng.tensor_mul(mu_re[:], tmp4[:, 0:128], tmp1[:, 0:128])
        return eng.tensor_mul(mu_im[:], tmp3[:, 0:128], tmp1[:, 0:128])
    stage("mu_v2", vector=mu_v2)

    mut_f = lambda: rf(O_MUT, 2048, 0, 8)
    for ri, src_t in ((0, mu_re), (1, mu_im)):
        for grp4 in range(4):
            def tmm(eng, g=grp4, src_t=src_t):
                last = None
                for i in range(4):
                    t = g * 4 + i
                    last = eng.transpose(pb[i][0:8, 0:128],
                                         src_t[:, t * 8:(t + 1) * 8], idn[:])
                return last
            stage(f"mu{ri}_t{grp4}", tensor=tmm)
            def tev(eng, g=grp4):
                last = None
                for i in range(4):
                    t = g * 4 + i
                    last = eng.tensor_copy(rf(O_MUT, 2048, 0, 8)[0:8, t * 128:(t + 1) * 128],
                                           pb[i][0:8, 0:128])
                return last
            stage(f"mu{ri}_e{grp4}", vector=tev)
        def mu_st(eng, ri=ri):
            dst = AP(tensor=muT_h.ap().tensor, offset=ri * N8 * H, ap=[[H, N8], [1, H]])
            dma(eng, "s", dst, rf(O_MUT, 2048, 0, 8))
            dwait(eng, "s")
        stage(f"mu{ri}_st", sync=mu_st)

    stage_f = lambda: rf(O_STAGE, 1024)
    for tt in range(32):
        def ld(eng, tt=tt):
            dma(eng, "s", stage_f(), x_in.ap()[3 + tt * 128: 3 + (tt + 1) * 128, :])
            dwait(eng, "s")
        stage(f"xfm_l{tt}", sync=ld)
        def tr(eng, tt=tt):
            last = None
            for kd in range(8):
                last = eng.transpose(pb[kd // 4][:, (kd % 4) * 128:(kd % 4) * 128 + 128],
                                     stage_f().rearrange("p (k f) -> p k f", k=8)[:, kd, :],
                                     idn[:])
            return last
        stage(f"xfm_t{tt}", tensor=tr)
        def ev(eng, tt=tt):
            last = None
            for kd in range(8):
                last = eng.tensor_copy(rb(O_XFM + kd * LT + 3 + tt * 128, 128),
                                       pb[kd // 4][:, (kd % 4) * 128:(kd % 4) * 128 + 128])
            return last
        stage(f"xfm_e{tt}", vector=ev)
    def ldh(eng):
        dma(eng, "s", stage_f()[0:3, :], x_in.ap()[0:3, :])
        dwait(eng, "s")
    stage("xfm_lh", sync=ldh)
    def trh(eng):
        last = None
        for kd in range(8):
            last = eng.transpose(pb[kd // 4][:, (kd % 4) * 128:(kd % 4) * 128 + 128],
                                 stage_f().rearrange("p (k f) -> p k f", k=8)[:, kd, :],
                                 idn[:])
        return last
    stage("xfm_th", tensor=trh)
    def evh(eng):
        last = None
        for kd in range(8):
            last = eng.tensor_copy(rb(O_XFM + kd * LT, 3),
                                   pb[kd // 4][:, (kd % 4) * 128:(kd % 4) * 128 + 3])
        return last
    stage("xfm_eh", vector=evh)
    def xfm_st(eng):
        dst = AP(tensor=x_fm_h.ap().tensor, offset=0,
                 ap=[[LT, 128], [128 * LT, 8], [1, LT]])
        dma(eng, "s", dst, rb(O_XFM, 8 * LT).rearrange("p (k l) -> p k l", k=8))
        dwait(eng, "s")
    stage("xfm_st", sync=xfm_st)

    for ct in range(32):
        def ld(eng, ct=ct):
            dma(eng, "s", stage_f(), w_in.ap()[ct * 128:(ct + 1) * 128, :])
            dwait(eng, "s")
        stage(f"wi_l{ct}", sync=ld)
        def tr(eng, ct=ct):
            last = None
            for kd in range(8):
                last = eng.transpose(pb[kd // 4][:, (kd % 4) * 128:(kd % 4) * 128 + 128],
                                     stage_f().rearrange("p (k f) -> p k f", k=8)[:, kd, :],
                                     idn[:])
            return last
        stage(f"wi_t{ct}", tensor=tr)
        def ev(eng, ct=ct):
            last = None
            for kd in range(8):
                last = eng.tensor_copy(rb(O_WINT + kd * 4096 + ct * 128, 128),
                                       pb[kd // 4][:, (kd % 4) * 128:(kd % 4) * 128 + 128])
            return last
        stage(f"wi_e{ct}", vector=ev)

    for ct in range(32):
        is_x = ct < 16
        def mmA(eng, ct=ct):
            last = None
            for tt in range(3):
                for kd in range(8):
                    last = eng.matmul(pb[tt][:, 0:512],
                                      rb(O_WINT + kd * 4096 + ct * 128, 128),
                                      rb(O_XFM + kd * LT + tt * 512, 512),
                                      start=(kd == 0), stop=(kd == 7))
            return last
        stage(f"xz{ct}_mmA", tensor=mmA)
        if is_x:
            def evA(eng, ct=ct):
                last = None
                for tt in range(3):
                    last = eng.tensor_copy(rf(O_CONV, 8195)[:, tt * 512:(tt + 1) * 512],
                                           pb[tt][:, 0:512])
                return last
            stage(f"xz{ct}_evA", vector=evA)
        else:
            def evA(eng, ct=ct):
                last = eng.activation(rb(O_ZTILE, 509), pb[0][:, 3:512], AF.Silu)
                for tt in range(1, 3):
                    last = eng.activation(rb(O_ZTILE + tt * 512 - 3, 512),
                                          pb[tt][:, 0:512], AF.Silu)
                return last
            stage(f"xz{ct}_evA", scalar=evA)
        def mmB(eng, ct=ct):
            last = None
            for tt in range(3, 6):
                for kd in range(8):
                    last = eng.matmul(pb[tt][:, 0:512],
                                      rb(O_WINT + kd * 4096 + ct * 128, 128),
                                      rb(O_XFM + kd * LT + tt * 512, 512),
                                      start=(kd == 0), stop=(kd == 7))
            return last
        stage(f"xz{ct}_mmB", tensor=mmB)
        def mmC(eng, ct=ct):
            last = None
            for tt in range(6, 8):
                for kd in range(8):
                    last = eng.matmul(pb[tt - 6][:, 0:512],
                                      rb(O_WINT + kd * 4096 + ct * 128, 128),
                                      rb(O_XFM + kd * LT + tt * 512, 512),
                                      start=(kd == 0), stop=(kd == 7))
            for kd in range(8):
                last = eng.matmul(pb[2][:, 0:3],
                                  rb(O_WINT + kd * 4096 + ct * 128, 128),
                                  rb(O_XFM + kd * LT + 4096, 3),
                                  start=(kd == 0), stop=(kd == 7))
            return last
        
        if is_x:
            def evB(eng, ct=ct):
                last = None
                for tt in range(3, 6):
                    last = eng.tensor_copy(rf(O_CONV, 8195)[:, tt * 512:(tt + 1) * 512],
                                           pb[tt][:, 0:512])
                return last
            stage(f"xz{ct}_evB", vector=evB)
            stage(f"xz{ct}_mmC", tensor=mmC)
            def evC(eng, ct=ct):
                last = None
                for tt in range(6, 8):
                    last = eng.tensor_copy(rf(O_CONV, 8195)[:, tt * 512:(tt + 1) * 512],
                                           pb[tt - 6][:, 0:512])
                return eng.tensor_copy(rf(O_CONV, 8195)[:, 4096:4099], pb[2][:, 0:3])
            stage(f"xz{ct}_evC", vector=evC)
            def cvf(eng, ct=ct):
                xc = rf(O_CONV, 8195)
                upre = xc[:, 4099:8195]
                eng.tensor_scalar_mul(upre, xc[:, 0:4096], p_cw[:, ct * 4:ct * 4 + 1])
                last = None
                for tap in range(1, 4):
                    last = eng.scalar_tensor_tensor(upre, xc[:, tap:tap + 4096],
                                                    p_cw[:, ct * 4 + tap:ct * 4 + tap + 1],
                                                    upre, ALU.mult, ALU.add)
                return last
            stage(f"xz{ct}_conv", vector=cvf)
            def silu_u(eng, ct=ct):
                return eng.activation(rb(O_ZTILE, 4096), rf(O_CONV, 8195)[:, 4099:8195],
                                      AF.Silu, bias=p_cb[:, ct:ct + 1])
            stage(f"xz{ct}_silu", scalar=silu_u)
            def st_u(eng, ct=ct):
                dma(eng, "s", u_h.ap()[ct * 128:(ct + 1) * 128, :], rb(O_ZTILE, 4096))
                dwait(eng, "s")
            stage(f"xz{ct}_stu", sync=st_u)
        else:
            def evB(eng, ct=ct):
                last = None
                for tt in range(3, 6):
                    last = eng.activation(rb(O_ZTILE + tt * 512 - 3, 512),
                                          pb[tt][:, 0:512], AF.Silu)
                return last
            stage(f"xz{ct}_evB", scalar=evB)
            stage(f"xz{ct}_mmC", tensor=mmC)
            def evC(eng, ct=ct):
                last = None
                for tt in range(6, 8):
                    last = eng.activation(rb(O_ZTILE + tt * 512 - 3, 512),
                                          pb[tt - 6][:, 0:512], AF.Silu)
                return eng.activation(rb(O_ZTILE + 4093, 3), pb[2][:, 0:3], AF.Silu)
            stage(f"xz{ct}_evC", scalar=evC)
            def st_z(eng, ct=ct):
                dma(eng, "s", zs_h.ap()[(ct - 16) * 128:(ct - 15) * 128, :],
                    rb(O_ZTILE, 4096))
                dwait(eng, "s")
            stage(f"xz{ct}_stz", sync=st_z)

    u_slab_h = nc.dram_tensor("u_slab_h", [4, 128, 128 * 128], BF16, kind="Internal")
    y_h = nc.dram_tensor("y_h", [H, LH], BF16, kind="Internal")
    cc_in = nc.dram_tensor("cc_in", [2, 128, 128], F32, kind="Internal")
    cc_out = nc.dram_tensor("cc_out", [NCORE, 2, 128, 128], F32, kind="Internal",
                            addr_space="Shared")
    ccs = st.enter_context(nc.semaphore("ccs"))

    O_UC = 0
    O_USL = 16400
    O_GRE = 32800
    O_GIM = 49200
    O_IPRE = 65600
    O_IPIM = 69700
    O_EINC = 73800
    O_SELB = 77900
    O_SPRE = 0
    O_SPIM = 16384
    O_USL3 = 32800
    O_T = 49200
    O_OPRE = 65600
    O_OPIM = 73800
    O_YC = 82000

    sA_re = mu_re
    sA_im = mu_im
    a_tile = sb("a_tile", [128, 128], F32)
    b_tile = sb("b_tile", [128, 128], F32)
    si_re = sb("si_re", [128, 128], F32)
    si_im = sb("si_im", [128, 128], F32)
    sel_sb = sb("sel_sb", [1, 16], F32)
    epsv = sb("epsv", [1, 8], F32)

    for blk in range(4):
        def uc_load(eng, b=blk):
            for sig in range(4):
                for g in range(4):
                    src = AP(tensor=u_h.ap().tensor,
                             offset=(32 * b + 128 * sig + 512 * g) * LH,
                             ap=[[32, 128], [LH, 32], [1, 32]])
                    dst = rb(O_UC + sig * 4096 + g * 32, 32 * 128).rearrange(
                        "p (gr m) -> p gr m", m=128)[:, :, 0:32]
                    dma(eng, "s", dst, src)
            dwait(eng, "s")
        stage(f"m1b{blk}_uc", sync=uc_load)

        def ip_zero(eng, b=blk):
            eng.memset(rb(O_IPRE, 4096), 0.0)
            return eng.memset(rb(O_IPIM, 4096), 0.0)
        stage(f"m1b{blk}_ipz", vector=ip_zero)

        def ip_gather(eng, b=blk):
            for ri, off in ((0, O_IPRE), (1, O_IPIM)):
                for g in range(4):
                    for sig in range(4):
                        dst = rb(off + sig * 32 * 32, 1024, p0=32 * g, pn=32).rearrange(
                            "m (gr c) -> m gr c", c=32)[:, :, 8 * g: 8 * g + 8]
                        src = AP(tensor=ipv_h.ap().tensor,
                                 offset=(32 * b + 128 * sig + 512 * g) * 512 + ri * 256,
                                 ap=[[8, 32], [512, 32], [1, 8]])
                        dma(eng, "s", dst, src)
            dwait(eng, "s")
        stage(f"m1b{blk}_ipg", sync=ip_gather)

        for q in range(8):
            def ttr(eng, b=blk, q=q):
                last = None
                for i in range(16):
                    sr = q * 16 + i
                    bank, slot = i // 8, i % 8
                    last = eng.transpose(pbt[bank][:, slot * 128:(slot + 1) * 128],
                                         rb(O_UC + sr * 128, 128), idnb[:])
                return last
            stage(f"m1b{blk}_tr{q}", tensor=ttr)
            def tev(eng, b=blk, q=q):
                last = None
                for i in range(16):
                    sr = q * 16 + i
                    bank, slot = i // 8, i % 8
                    last = eng.tensor_copy(rb(O_USL + sr * 128, 128),
                                           pbt[bank][:, slot * 128:(slot + 1) * 128])
                return last
            stage(f"m1b{blk}_te{q}", vector=tev)
        def usl_store(eng, b=blk):
            dma(eng, "s", u_slab_h.ap()[b][:, 0:8192], rb(O_USL, 8192))
            dwait(eng, "s")
        def usl_store_g(eng, b=blk):
            dma(eng, "g", u_slab_h.ap()[b][:, 8192:16384], rb(O_USL + 8192, 8192))
            dwait(eng, "g")
        stage(f"m1b{blk}_us", sync=usl_store, gpsimd=usl_store_g)

        for q in range(4):
            def ipmm(eng, b=blk, q=q):
                last = None
                for i in range(8):
                    gr = q * 8 + i
                    bank, slot = i // 4, i % 4
                    for sig in range(4):
                        sr = sig * 32 + gr
                        for ri, off in ((0, O_IPRE), (1, O_IPIM)):
                            ps = pb[bank if ri == 0 else bank + 2]
                            last = eng.matmul(
                                ps[32 * sig:32 * sig + 32, slot * 128:(slot + 1) * 128],
                                rb(off + sr * 32, 32),
                                rb(O_USL + sr * 128, 128),
                                start=True, stop=True,
                                tile_position=(0, 32 * sig))
                return last
            stage(f"m1b{blk}_ip{q}", tensor=ipmm)
            def ipev(eng, b=blk, q=q):
                last = None
                for i in range(8):
                    gr = q * 8 + i
                    G = 32 * b + gr
                    bank, slot = i // 4, i % 4
                    dre = arena[:, O_GRE + G: O_GRE + G + CC * 128: 128]
                    dim_ = arena[:, O_GIM + G: O_GIM + G + CC * 128: 128]
                    src_re = pb[bank][:, slot * 128:(slot + 1) * 128]
                    src_im = pb[bank + 2][:, slot * 128:(slot + 1) * 128]
                    eng.tensor_copy(dre, src_re)
                    last = eng.tensor_copy(dim_, src_im)
                return last
            stage(f"m1b{blk}_ie{q}", vector=ipev)

    def sc_ab(eng):
        for ri, dst in ((0, a_tile), (1, b_tile)):
            for sig in range(4):
                for g in range(4):
                    src = AP(tensor=muT_h.ap().tensor,
                             offset=ri * N8 * H + 128 * sig + 512 * g,
                             ap=[[H, N8], [1, 128]])
                    dma(eng, "s", dst[32 * sig + 8 * g: 32 * sig + 8 * g + 8, :], src)
        dma(eng, "s", sel_sb[0:1, 0:8], sel.ap())
        dwait(eng, "s")
    stage("sc_ab", sync=sc_ab)

    selb_bf = sb("selb_bf", [1, 16], BF16)
    def selb_cast(eng):
        return eng.tensor_copy(selb_bf[0:1, 0:8], sel_sb[0:1, 0:8])
    stage("selb_cast", vector=selb_cast)
    def selb_mm2(eng):
        return eng.matmul(pb[0][:, 0:8], onesb[0:1, :], selb_bf[0:1, 0:8],
                          start=True, stop=True)
    stage("selb_mm", tensor=selb_mm2)
    selb = sb("selb", [128, 8], F32)
    def selb_ev(eng):
        return eng.tensor_copy(selb[:], pb[0][:, 0:8])
    stage("selb_ev", vector=selb_ev)

    def scan_steps(eng, write_out):
        t_re = tmp1[:, 0:128]
        t_im = tmp2[:, 0:128]
        last = None
        for c in range(CC):
            gre = rb(O_GRE + c * 128, 128)
            gim = rb(O_GIM + c * 128, 128)
            eng.tensor_mul(t_re, a_tile[:], sA_re[:])
            eng.tensor_mul(tmp3[:, 0:128], b_tile[:], sA_im[:])
            eng.tensor_sub(t_re, t_re, tmp3[:, 0:128])
            eng.tensor_tensor(t_re, t_re, gre, ALU.add)
            eng.tensor_mul(t_im, b_tile[:], sA_re[:])
            eng.tensor_mul(tmp3[:, 0:128], a_tile[:], sA_im[:])
            eng.tensor_add(t_im, t_im, tmp3[:, 0:128])
            eng.tensor_tensor(t_im, t_im, gim, ALU.add)
            eng.tensor_copy(sA_re[:], t_re)
            last = eng.tensor_copy(sA_im[:], t_im)
            if write_out and c < CC - 1:
                eng.tensor_copy(rb(O_GRE + (c + 1) * 128, 128), sA_re[:])
                last = eng.tensor_copy(rb(O_GIM + (c + 1) * 128, 128), sA_im[:])
        return last

    def scanA_init(eng):
        eng.memset(sA_re[:], 0.0)
        return eng.memset(sA_im[:], 0.0)
    stage("scanA_init", vector=scanA_init)
    def scanA(eng):
        return scan_steps(eng, False)
    stage("scanA", vector=scanA)

    def e_pre(eng):
        dma(eng, "s", cc_in.ap()[0], sA_re[:])
        dma(eng, "s", cc_in.ap()[1], sA_im[:])
        dwait(eng, "s")
    stage("e_pre", sync=e_pre)

    def e_cc(eng):
        eng.collective_compute("AllGather", ALU.bypass, [list(range(NCORE))],
                               ins=[cc_in.ap()], outs=[cc_out.ap()]).then_inc(ccs, 1)
        eng.wait_ge(ccs, 1)
        return None
    stage("e_cc", gpsimd=e_cc)

    def e_post(eng):
        src = AP(tensor=cc_out.ap().tensor, offset=0,
                 ap=[[128, 128], [2 * 128 * 128, NCORE], [128 * 128, 2], [1, 128]])
        dma(eng, "s", rf(O_EINC, 2048).rearrange("p (r i g) -> p r i g", r=8, i=2), src)
        dwait(eng, "s")
    stage("e_post", sync=e_post)

    def s_init(eng):
        eng.memset(si_re[:], 0.0)
        eng.memset(si_im[:], 0.0)
        last = None
        for r in range(NCORE):
            e4 = rf(O_EINC, 2048).rearrange("p (r i g) -> p r i g", r=8, i=2)
            eng.scalar_tensor_tensor(si_re[:], e4[:, r, 0, :], selb[:, r:r + 1],
                                     si_re[:], ALU.mult, ALU.add)
            last = eng.scalar_tensor_tensor(si_im[:], e4[:, r, 1, :], selb[:, r:r + 1],
                                            si_im[:], ALU.mult, ALU.add)
        eng.tensor_copy(sA_re[:], si_re[:])
        return eng.tensor_copy(sA_im[:], si_im[:])
    stage("s_init", vector=s_init)

    def scanB(eng):
        t_re = tmp1[:, 0:128]
        t_im = tmp2[:, 0:128]
        eng.tensor_copy(rb(O_SPRE, 128), sA_re[:])
        eng.tensor_copy(rb(O_SPIM, 128), sA_im[:])
        last = None
        for c in range(CC):
            gre = rb(O_GRE + c * 128, 128)
            gim = rb(O_GIM + c * 128, 128)
            eng.tensor_mul(t_re, a_tile[:], sA_re[:])
            eng.tensor_mul(tmp3[:, 0:128], b_tile[:], sA_im[:])
            eng.tensor_sub(t_re, t_re, tmp3[:, 0:128])
            eng.tensor_tensor(t_re, t_re, gre, ALU.add)
            eng.tensor_mul(t_im, b_tile[:], sA_re[:])
            eng.tensor_mul(tmp3[:, 0:128], a_tile[:], sA_im[:])
            eng.tensor_add(t_im, t_im, tmp3[:, 0:128])
            eng.tensor_tensor(t_im, t_im, gim, ALU.add)
            eng.tensor_copy(sA_re[:], t_re)
            last = eng.tensor_copy(sA_im[:], t_im)
            if c < CC - 1:
                eng.tensor_copy(rb(O_SPRE + (c + 1) * 128, 128), sA_re[:])
                last = eng.tensor_copy(rb(O_SPIM + (c + 1) * 128, 128), sA_im[:])
        return last
    stage("scanB", vector=scanB)

    def t_gather(eng):
        for g in range(4):
            dst = rb(O_T, 16384, p0=32 * g, pn=32).rearrange("m (j s) -> m j s", s=512)
            src = AP(tensor=kpT_h.ap().tensor, offset=31 * H + 512 * g,
                     ap=[[H, 32], [-H, 32], [1, 512]])
            dma(eng, "s", dst, src)
        dwait(eng, "s")
    stage("t_gather", sync=t_gather)

    for blk in range(4):
        def usl_load(eng, b=blk):
            dma(eng, "s", rb(O_USL3, 8192), u_slab_h.ap()[b][:, 0:8192])
            dwait(eng, "s")
        def usl_load_g(eng, b=blk):
            dma(eng, "g", rb(O_USL3 + 8192, 8192), u_slab_h.ap()[b][:, 8192:16384])
            dwait(eng, "g")
        stage(f"m3b{blk}_ul", sync=usl_load, gpsimd=usl_load_g)

        for hf in range(2):
            def op_zero(eng, b=blk, hf=hf):
                eng.memset(rb(O_OPRE, 8192), 0.0)
                return eng.memset(rb(O_OPIM, 8192), 0.0)
            stage(f"m3b{blk}h{hf}_opz", vector=op_zero)

            def _opg_emit(eng, q, b, hf, ri, off):
                for s2 in range(2):
                    sig = 2 * hf + s2
                    for g in range(4):
                        p0 = 32 * sig + 8 * g
                        dst = rb(off + s2 * 32 * 128, 32 * 128,
                                 p0=p0, pn=8).rearrange(
                                 "n (gr r) -> n gr r", r=128)[:, :, 32 * g:32 * g + 32]
                        srcp = AP(tensor=opv_h.ap().tensor,
                                  offset=(32 * b + 128 * sig + 512 * g) * 512
                                         + ri * 256,
                                  ap=[[32, 8], [512, 32], [1, 32]])
                        dma(eng, q, dst, srcp)
                dwait(eng, q)
            def op_gather(eng, b=blk, hf=hf, _f=_opg_emit):
                _f(eng, "s", b, hf, 0, O_OPRE)
            def op_gather_g(eng, b=blk, hf=hf, _f=_opg_emit):
                _f(eng, "g", b, hf, 1, O_OPIM)
            stage(f"m3b{blk}h{hf}_opg", sync=op_gather, gpsimd=op_gather_g)

            for q in range(4):
                def ymm(eng, b=blk, hf=hf, q=q):
                    last = None
                    for i in range(16):
                        srh = q * 16 + i
                        s2, gr = srh // 32, srh % 32
                        sig = 2 * hf + s2
                        sr = sig * 32 + gr
                        s_global = 32 * b + gr + 128 * sig
                        G = 32 * b + gr
                        bank, slot = i // 4, i % 4
                        ps = pb[bank]
                        col = slice(slot * 128, (slot + 1) * 128)
                        for g in range(4):
                            lhsT = rb(O_T, 16384, p0=32 * g, pn=32).rearrange(
                                "m (j s) -> m j s", s=512)[:, :, s_global]
                            rhs = rb(O_USL3 + sr * 128, 128, p0=32 * g, pn=32)
                            last = eng.matmul(
                                ps[32 * g:32 * g + 32, col],
                                lhsT, rhs, start=True, stop=False,
                                tile_position=(32 * g, 32 * g),
                                skip_group_check=True)
                        rhs_re = arena[:, O_SPRE + G: O_SPRE + G + CC * 128: 128]
                        rhs_im = arena[:, O_SPIM + G: O_SPIM + G + CC * 128: 128]
                        last = eng.matmul(ps[:, col], rb(O_OPRE + srh * 128, 128),
                                          rhs_re, start=False, stop=False,
                                          skip_group_check=True)
                        last = eng.matmul(ps[:, col], rb(O_OPIM + srh * 128, 128),
                                          rhs_im, start=False, stop=True,
                                          skip_group_check=True)
                    return last
                stage(f"m3b{blk}h{hf}_y{q}", tensor=ymm)
                def yev(eng, b=blk, hf=hf, q=q):
                    last = None
                    for i in range(16):
                        bank, slot = i // 4, i % 4
                        last = eng.tensor_copy(rb(O_YC + i * 128, 128),
                                               pb[bank][:, slot * 128:(slot + 1) * 128])
                    return last
                stage(f"m3b{blk}h{hf}_ye{q}", vector=yev)
                def ytr(eng, b=blk, hf=hf, q=q):
                    last = None
                    for i in range(16):
                        bank, slot = i // 8, i % 8
                        last = eng.transpose(pbt[bank][:, slot * 128:(slot + 1) * 128],
                                             rb(O_YC + i * 128, 128), idnb[:])
                    return last
                stage(f"m3b{blk}h{hf}_yt{q}", tensor=ytr)
                def yce(eng, b=blk, hf=hf, q=q):
                    last = None
                    for i in range(16):
                        bank, slot = i // 8, i % 8
                        last = eng.tensor_copy(rb(O_YC + i * 128, 128),
                                               pbt[bank][:, slot * 128:(slot + 1) * 128])
                    return last
                stage(f"m3b{blk}h{hf}_yc{q}", vector=yce)
                def yst(eng, b=blk, hf=hf, q=q):
                    for i in range(16):
                        srh = q * 16 + i
                        s2, gr = srh // 32, srh % 32
                        sig = 2 * hf + s2
                        s_global = 32 * b + gr + 128 * sig
                        dst = AP(tensor=y_h.ap().tensor, offset=s_global * LH,
                                 ap=[[32, 128], [512 * LH, 4], [1, 32]])
                        dma(eng, "s", dst, rb(O_YC + i * 128, 128).rearrange(
                            "p (g m) -> p g m", g=4))
                    dwait(eng, "s")
                stage(f"m3b{blk}h{hf}_ys{q}", sync=yst)

    O_WOT = 0
    O_YGH = 16400
    O_STR = 49200
    O_WST = 82000
    h_all_h = nc.dram_tensor("h_all_h", [EXP, INTER, LH], BF16, kind="Internal")
    comb_h = nc.dram_tensor("comb_h", [EXP, LH], BF16, kind="Internal")
    we2b_h = nc.dram_tensor("we2b_h", [EXP * INTER, DM], BF16, kind="Internal")
    zsh_h = nc.dram_tensor("zsh_h", [DM, LH], BF16, kind="Internal")
    preln2_h = nc.dram_tensor("preln2_h", [DM, LH], F32, kind="Internal")
    mrb = sb("mrb", [1, 1024], BF16)
    bsv = sb("bsv", [128, 24], F32)
    bev = sb("bev", [128, 128], F32)
    be2b = sb("be2b", [16, 1024], BF16)
    gbx = sb("gbx", [128, 16], F32)
    sem_c = st.enter_context(nc.semaphore("sem_c"))
    sem_v = st.enter_context(nc.semaphore("sem_v"))
    sem_hd = st.enter_context(nc.semaphore("sem_hd"))
    cnt = {"t": 0, "c": 0, "v": 0, "hd": 0}

    for dt in range(8):
        def ld(eng, dt=dt):
            dma(eng, "s", rf(O_WST, 2048), w_out.ap()[dt * 128:(dt + 1) * 128, :])
            dwait(eng, "s")
        stage(f"wo{dt}_l", sync=ld)
        for hh in range(2):
            def tr(eng, dt=dt, hh=hh):
                last = None
                for k in range(8):
                    ht = hh * 8 + k
                    last = eng.transpose(pb[k // 4][:, (k % 4) * 128:(k % 4) * 128 + 128],
                                         rf(O_WST, 2048)[:, ht * 128:(ht + 1) * 128],
                                         idn[:])
                return last
            stage(f"wo{dt}_t{hh}", tensor=tr)
            def ev(eng, dt=dt, hh=hh):
                last = None
                for k in range(8):
                    ht = hh * 8 + k
                    last = eng.tensor_copy(rb(O_WOT + ht * 1024 + dt * 128, 128),
                                           pb[k // 4][:, (k % 4) * 128:(k % 4) * 128 + 128])
                return last
            stage(f"wo{dt}_e{hh}", vector=ev)

    for th in range(2):
        for ct in range(16):
            def ygl(eng, th=th, ct=ct):
                dma(eng, "s", rb(O_WST, 2048),
                    y_h.ap()[ct * 128:(ct + 1) * 128, th * 2048:(th + 1) * 2048])
                dma(eng, "s", rb(O_WST + 2048, 2048),
                    zs_h.ap()[ct * 128:(ct + 1) * 128, th * 2048:(th + 1) * 2048])
                dwait(eng, "s")
            stage(f"w{th}_ygl{ct}", sync=ygl)
            def ygc(eng, th=th, ct=ct):
                return eng.tensor_tensor(rb(O_YGH + ct * 2048, 2048),
                                         rb(O_WST, 2048), rb(O_WST + 2048, 2048),
                                         ALU.mult)
            stage(f"w{th}_ygc{ct}", vector=ygc)
        for tt4 in range(4):
            def xl(eng, th=th, tt4=tt4):
                for kd in range(8):
                    src = AP(tensor=x_fm_h.ap().tensor,
                             offset=kd * 128 * LT + 3 + th * 2048 + tt4 * 512,
                             ap=[[LT, 128], [1, 512]])
                    dma(eng, "s", rb(O_WST + kd * 512, 512), src)
                dwait(eng, "s")
            stage(f"w{th}t{tt4}_xl", sync=xl)
            for half in range(2):
                dts = range(6) if half == 0 else range(6, 8)
                def mm(eng, th=th, tt4=tt4, dts=tuple(dts)):
                    last = None
                    for j, dt in enumerate(dts):
                        for ct in range(16):
                            last = eng.matmul(
                                pb[j][:, 0:512],
                                rb(O_WOT + ct * 1024 + dt * 128, 128),
                                rb(O_YGH + ct * 2048 + tt4 * 512, 512),
                                start=(ct == 0), stop=(ct == 15))
                    return last
                stage(f"w{th}t{tt4}_mm{half}", tensor=mm)
                def ev(eng, th=th, tt4=tt4, dts=tuple(dts)):
                    last = None
                    for j, dt in enumerate(dts):
                        last = eng.tensor_tensor(
                            rb(O_STR + dt * 4096 + th * 2048 + tt4 * 512, 512),
                            pb[j][:, 0:512], rb(O_WST + dt * 512, 512), ALU.add)
                    return last
                stage(f"w{th}t{tt4}_ev{half}", vector=ev)

    def ln_stages(tag, get_tile, n_tt, apply_dst=None, src_f32=False):
        for tt in range(n_tt):
            def sq(eng, tt=tt):
                last = None
                for kd in range(8):
                    last = eng.activation(rb(O_WST + kd * 512, 512),
                                          get_tile(kd, tt), AF.Square)
                return last
            stage(f"{tag}{tt}_sq", scalar=sq)
            if src_f32:
                def cst(eng, tt=tt):
                    last = None
                    for kd in range(8):
                        last = eng.tensor_copy(rb(O_WST + 4096 + kd * 512, 512),
                                               get_tile(kd, tt))
                    return last
                stage(f"{tag}{tt}_cs", vector=cst)
            def smm(eng, tt=tt):
                last = None
                for kd in range(8):
                    rhs = (rb(O_WST + 4096 + kd * 512, 512) if src_f32
                           else get_tile(kd, tt))
                    last = eng.matmul(pb[4][0:1, 0:512], onesb[:, 0:1], rhs,
                                      start=(kd == 0), stop=(kd == 7))
                for kd in range(8):
                    last = eng.matmul(pb[5][0:1, 0:512], onesb[:, 0:1],
                                      rb(O_WST + kd * 512, 512),
                                      start=(kd == 0), stop=(kd == 7))
                return last
            stage(f"{tag}{tt}_mm", tensor=smm)
            def stat(eng, tt=tt):
                m = tmp1[0:1, 0:512]
                v = tmp2[0:1, 0:512]
                eng.tensor_scalar_mul(m, pb[4][0:1, 0:512], 1.0 / 1024.0)
                eng.tensor_scalar_mul(v, pb[5][0:1, 0:512], 1.0 / 1024.0)
                t = tmp3[0:1, 0:512]
                eng.tensor_mul(t, m, m)
                eng.tensor_sub(v, v, t)
                return eng.tensor_copy(mrb[0:1, 0:512], m)
            stage(f"{tag}{tt}_st", vector=stat)
            def rst(eng, tt=tt):
                return eng.activation(tmp2[0:1, 0:512], tmp2[0:1, 0:512],
                                      AF.Sqrt, bias=epsv[0:1, 0:1])
            stage(f"{tag}{tt}_rs", scalar=rst)
            def rstc(eng, tt=tt):
                eng.reciprocal(tmp2[0:1, 0:512], tmp2[0:1, 0:512])
                return eng.tensor_copy(mrb[0:1, 512:1024], tmp2[0:1, 0:512])
            stage(f"{tag}{tt}_rc", vector=rstc)
            def bmm(eng, tt=tt):
                eng.matmul(pb[4][:, 0:512], onesb[0:1, :], mrb[0:1, 0:512],
                           start=True, stop=True)
                return eng.matmul(pb[5][:, 0:512], onesb[0:1, :], mrb[0:1, 512:1024],
                                  start=True, stop=True)
            stage(f"{tag}{tt}_bm", tensor=bmm)
            def appl(eng, tt=tt):
                last = None
                for kd in range(8):
                    t = rf(O_WST, 512)
                    eng.tensor_tensor(t, get_tile(kd, tt), pb[4][:, 0:512], ALU.subtract)
                    dst = apply_dst(kd, tt) if apply_dst else get_tile(kd, tt)
                    last = eng.tensor_tensor(dst, t, pb[5][:, 0:512], ALU.mult)
                return last
            stage(f"{tag}{tt}_ap", vector=appl)

    ln_stages("ln1_", lambda kd, tt: rb(O_STR + kd * 4096 + tt * 512, 512), 8)

    O_GWT = 0
    O_SCF = 300
    O_SCT = 8600
    O_ORIG = 9700
    O_CTM = 10800
    O_CBF = 82100

    def gb_load(eng):
        for p in range(128):
            src = AP(tensor=inp["gate_b"].ap().tensor, offset=0, ap=[[16, 1], [1, 16]])
            dma(eng, "s", gbx[p:p + 1, 0:16], src)
        dma(eng, "s", rf(O_WST, 1024, p0=0, pn=16), gate_w.ap())
        dwait(eng, "s")
    stage("g_ld", sync=gb_load)

    def gw_tr(eng):
        last = None
        for kd in range(8):
            last = eng.transpose(pb[kd // 4][:, (kd % 4) * 128:(kd % 4) * 128 + 16],
                                 rf(O_WST, 1024, p0=0, pn=16)[:, kd * 128:(kd + 1) * 128],
                                 idn[0:16, 0:16])
        return last
    stage("g_tr", tensor=gw_tr)
    def gw_ev(eng):
        last = None
        for kd in range(8):
            last = eng.tensor_copy(rb(O_GWT + kd * 16, 16, p0=0, pn=128),
                                   pb[kd // 4][0:128, (kd % 4) * 128:(kd % 4) * 128 + 16])
        return last
    stage("g_ev", vector=gw_ev)

    def g_mm(eng):
        last = None
        for tt in range(8):
            for kd in range(8):
                last = eng.matmul(pb[tt % 4][0:16, 0:512],
                                  rb(O_GWT + kd * 16, 16),
                                  rb(O_STR + kd * 4096 + tt * 512, 512),
                                  start=(kd == 0), stop=(kd == 7))
            if tt % 4 == 3:
                pass
        return last
    for q in range(2):
        def gmm(eng, q=q):
            last = None
            for i in range(4):
                tt = q * 4 + i
                for kd in range(8):
                    last = eng.matmul(pb[i][0:16, 0:512],
                                      rb(O_GWT + kd * 16, 16),
                                      rb(O_STR + kd * 4096 + tt * 512, 512),
                                      start=(kd == 0), stop=(kd == 7))
            return last
        stage(f"g_mm{q}", tensor=gmm)
        def gev(eng, q=q):
            last = None
            for i in range(4):
                tt = q * 4 + i
                last = eng.tensor_copy(rf(O_SCF, 4096, p0=0, pn=16)[:, tt * 512:(tt + 1) * 512],
                                       pb[i][0:16, 0:512])
            return last
        stage(f"g_ev{q}", vector=gev)

    for q in range(4):
        def str_(eng, q=q):
            last = None
            for i in range(8):
                tl = q * 8 + i
                last = eng.transpose(pb[i // 4][0:128, (i % 4) * 128:(i % 4) * 128 + 16],
                                     rf(O_SCF, 4096, p0=0, pn=16)[:, tl * 128:(tl + 1) * 128],
                                     idn[0:16, 0:16])
            return last
        stage(f"g_st{q}", tensor=str_)
        def sev(eng, q=q):
            last = None
            for i in range(8):
                tl = q * 8 + i
                last = eng.tensor_copy(rf(O_SCT, 1024)[:, tl * 16:(tl + 1) * 16],
                                       pb[i // 4][0:128, (i % 4) * 128:(i % 4) * 128 + 16])
            return last
        stage(f"g_se{q}", vector=sev)

    def g_smax_v1(eng):
        sc = rf(O_SCT, 1024)
        og = rf(O_ORIG, 1024)
        last = None
        for tl in range(32):
            eng.tensor_reduce(tmp1[:, tl:tl + 1], sc[:, tl * 16:(tl + 1) * 16],
                              mybir.AxisListType.X, ALU.max)
        for tl in range(32):
            last = eng.tensor_tensor(og[:, tl * 16:(tl + 1) * 16],
                                     sc[:, tl * 16:(tl + 1) * 16],
                                     tmp1[:, tl:tl + 1].broadcast_to((128, 16)),
                                     ALU.subtract)
        return last
    stage("g_sm1", vector=g_smax_v1)
    def g_smax_c(eng):
        og = rf(O_ORIG, 1024)
        return eng.activation(og[:, 0:512], og[:, 0:512], AF.Exp)
    stage("g_sm2", scalar=g_smax_c)
    def g_smax_v2(eng):
        og = rf(O_ORIG, 1024)
        for tl in range(32):
            eng.tensor_reduce(tmp2[:, tl:tl + 1], og[:, tl * 16:(tl + 1) * 16],
                              mybir.AxisListType.X, ALU.add)
        eng.engine_nop()
        return eng.reciprocal(tmp2[:, 0:32], tmp2[:, 0:32])
    stage("g_sm3", vector=g_smax_v2)
    def g_smax_v3(eng):
        og = rf(O_ORIG, 1024)
        last = None
        for tl in range(32):
            last = eng.tensor_tensor(og[:, tl * 16:(tl + 1) * 16],
                                     og[:, tl * 16:(tl + 1) * 16],
                                     tmp2[:, tl:tl + 1].broadcast_to((128, 16)),
                                     ALU.mult)
        return last
    stage("g_sm4", vector=g_smax_v3)

    def g_route(eng):
        og = rf(O_ORIG, 1024)
        sc = rf(O_SCT, 1024)
        ctm = rf(O_CTM, 1024)
        gsc_a = tmp3[:, 0:128]
        thr_a = tmp1[:, 32:64]
        keep_a = tmp3[:, 128:256]
        mterm_a = tmp3[:, 256:384]
        sgm_a = tmp4[:, 0:512]
        t1_a = tmp1[:, 64:96]
        t2_a = tmp1[:, 96:128]
        sel1_a = tmp2[:, 0:512]
        sg2_a = rrx[:, 0:512]
        for tl in range(32):
            eng.tensor_add(sc[:, tl * 16:(tl + 1) * 16], og[:, tl * 16:(tl + 1) * 16],
                           gbx[:, 0:16])
        def mem(j):
            return rf(O_SCT, 512)[:, j::4]
        m1_a = tmp1[:, 128:256]; n1_a = tmp1[:, 256:384]
        m2_a = tmp1[:, 384:512]; n2_a = kk_built[:, 0:128]
        hi_a = tmp2[:, 0:128]; se_a = tmp2[:, 128:256]; nx_a = tmp2[:, 256:384]
        eng.tensor_tensor(m1_a, mem(0), mem(1), ALU.max)
        eng.tensor_tensor(n1_a, mem(0), mem(1), ALU.min)
        eng.tensor_tensor(m2_a, mem(2), mem(3), ALU.max)
        eng.tensor_tensor(n2_a, mem(2), mem(3), ALU.min)
        eng.tensor_tensor(hi_a, m1_a, m2_a, ALU.max)
        eng.tensor_tensor(se_a, m1_a, m2_a, ALU.min)
        eng.tensor_tensor(nx_a, n1_a, n2_a, ALU.max)
        eng.tensor_tensor(se_a, se_a, nx_a, ALU.max)
        eng.tensor_add(gsc_a, hi_a, se_a)
        def grp(j):
            return gsc_a[:, j::4]
        M1_a = kk_built[:, 128:160]; N1_a = kk_built[:, 160:192]
        M2_a = kk_built[:, 192:224]; N2_a = kk_built[:, 224:256]
        u_a = kk_built[:, 256:288]; w_a = kk_built[:, 288:320]
        eng.tensor_tensor(M1_a, grp(0), grp(1), ALU.max)
        eng.tensor_tensor(N1_a, grp(0), grp(1), ALU.min)
        eng.tensor_tensor(M2_a, grp(2), grp(3), ALU.max)
        eng.tensor_tensor(N2_a, grp(2), grp(3), ALU.min)
        eng.tensor_copy(rrx[:, 0:512], tmp4[:, 0:512])
        eng.tensor_tensor(u_a, M1_a, M2_a, ALU.min)
        eng.tensor_tensor(w_a, N1_a, N2_a, ALU.max)
        eng.tensor_copy(rrx[:, 0:512], tmp4[:, 0:512])
        eng.tensor_tensor(thr_a, u_a, w_a, ALU.max)
        eng.tensor_copy(rrx[:, 0:512], tmp4[:, 0:512])
        for tl in range(32):
            eng.tensor_tensor(keep_a[:, tl * 4:(tl + 1) * 4],
                              gsc_a[:, tl * 4:(tl + 1) * 4],
                              thr_a[:, tl:tl + 1].broadcast_to((128, 4)), ALU.is_ge)
        for tl in range(32):
            eng.tensor_scalar(mterm_a[:, tl * 4:(tl + 1) * 4],
                              keep_a[:, tl * 4:(tl + 1) * 4], 1e9, -1e9,
                              ALU.mult, ALU.add)
        eng.tensor_copy(rrx[:, 0:512], tmp4[:, 0:512])
        for tl in range(32):
            for g in range(4):
                eng.tensor_tensor(sgm_a[:, tl * 16 + 4 * g:tl * 16 + 4 * g + 4],
                                  sc[:, tl * 16 + 4 * g:tl * 16 + 4 * g + 4],
                                  mterm_a[:, tl * 4 + g:tl * 4 + g + 1].broadcast_to((128, 4)),
                                  ALU.add)
        for tl in range(32):
            eng.tensor_reduce(t1_a[:, tl:tl + 1], sgm_a[:, tl * 16:(tl + 1) * 16],
                              mybir.AxisListType.X, ALU.max)
        eng.tensor_copy(kk_built[:, 0:512], tmp4[:, 0:512])
        for tl in range(32):
            eng.tensor_tensor(sel1_a[:, tl * 16:(tl + 1) * 16],
                              sgm_a[:, tl * 16:(tl + 1) * 16],
                              t1_a[:, tl:tl + 1].broadcast_to((128, 16)), ALU.is_ge)
        for tl in range(32):
            eng.scalar_tensor_tensor(sg2_a[:, tl * 16:(tl + 1) * 16],
                                     sel1_a[:, tl * 16:(tl + 1) * 16], -1e9,
                                     sgm_a[:, tl * 16:(tl + 1) * 16],
                                     ALU.mult, ALU.add)
        for tl in range(32):
            eng.tensor_reduce(t2_a[:, tl:tl + 1], sg2_a[:, tl * 16:(tl + 1) * 16],
                              mybir.AxisListType.X, ALU.max)
        eng.tensor_copy(kk_built[:, 0:512], tmp4[:, 0:512])
        for tl in range(32):
            eng.tensor_tensor(sel1_a[:, tl * 16:(tl + 1) * 16],
                              sgm_a[:, tl * 16:(tl + 1) * 16],
                              t2_a[:, tl:tl + 1].broadcast_to((128, 16)), ALU.is_ge)
        last = None
        for tl in range(32):
            last = eng.tensor_tensor(ctm[:, tl * 16:(tl + 1) * 16],
                                     og[:, tl * 16:(tl + 1) * 16],
                                     sel1_a[:, tl * 16:(tl + 1) * 16], ALU.mult)
        return last
    stage("g_route", vector=g_route)

    for q in range(4):
        def ctr(eng, q=q):
            last = None
            for i in range(8):
                tl = q * 8 + i
                last = eng.transpose(pb[i // 4][0:16, (i % 4) * 128:(i % 4) * 128 + 128],
                                     rf(O_CTM, 1024)[:, tl * 16:(tl + 1) * 16],
                                     idn[:])
            return last
        stage(f"g_ct{q}", tensor=ctr)
        def cev(eng, q=q):
            last = None
            for i in range(8):
                tl = q * 8 + i
                last = eng.tensor_copy(
                    rb(O_CBF, 4096, p0=0, pn=16)[:, tl * 128:(tl + 1) * 128],
                    pb[i // 4][0:16, (i % 4) * 128:(i % 4) * 128 + 128])
            return last
        stage(f"g_cv{q}", vector=cev)

    def comb_st(eng):
        dma(eng, "s", comb_h.ap(), rb(O_CBF, 4096, p0=0, pn=16))
        dwait(eng, "s")
    stage("comb_st", sync=comb_st)

    def bias_ld(eng):
        for t in range(8):
            for nm, col in (("bs1", 0), ("bs3", 8), ("bs2", 16)):
                src = AP(tensor=inp[nm].ap().tensor, offset=t * 128, ap=[[1, 128], [1, 1]])
                dma(eng, "s", bsv[:, col + t:col + t + 1], src)
        for e in range(EXP):
            for it in range(4):
                for nm, col in (("be1", 0), ("be3", 64)):
                    src = AP(tensor=inp[nm].ap().tensor, offset=e * 512 + it * 128,
                             ap=[[1, 128], [1, 1]])
                    dma(eng, "s", bev[:, col + e * 4 + it:col + e * 4 + it + 1], src)
        dma(eng, "s", rf(320, 1024, p0=0, pn=16), inp["be2"].ap())
        dwait(eng, "s")
    stage("bias_ld", sync=bias_ld)
    def be2_cast(eng):
        return eng.tensor_copy(be2b[:], rf(320, 1024, p0=0, pn=16))
    stage("be2_cast", vector=be2_cast)

    O_WS1 = 0
    O_WS3 = 8300
    O_SST = 16600
    O_ZST = 24000

    for w_i, (nm, off) in enumerate((("ws1", O_WS1), ("ws3", O_WS3))):
        for kd in range(8):
            def ldw(eng, nm=nm, kd=kd):
                dma(eng, "s", rf(O_SST, 1024), inp[nm].ap()[kd * 128:(kd + 1) * 128, :])
                dwait(eng, "s")
            stage(f"sh_{nm}l{kd}", sync=ldw)
            def cst(eng, off=off, kd=kd):
                return eng.tensor_copy(rb(off + kd * 1024, 1024), rf(O_SST, 1024))
            stage(f"sh_{nm}c{kd}", vector=cst)

    for i2t in range(8):
        for r3 in range(3):
            tts = [r3 * 3 + k for k in range(3) if r3 * 3 + k < 8]
            def shmm(eng, i2t=i2t, tts=tuple(tts)):
                last = None
                for j, tt in enumerate(tts):
                    for kd in range(8):
                        last = eng.matmul(pb[j][:, 0:512],
                                          rb(O_WS1 + kd * 1024 + i2t * 128, 128),
                                          rb(O_STR + kd * 4096 + tt * 512, 512),
                                          start=(kd == 0), stop=(kd == 7))
                        last = eng.matmul(pb[3 + j][:, 0:512],
                                          rb(O_WS3 + kd * 1024 + i2t * 128, 128),
                                          rb(O_STR + kd * 4096 + tt * 512, 512),
                                          start=(kd == 0), stop=(kd == 7))
                    pass
                return last
            stage(f"sh{i2t}_m{r3}", tensor=shmm)
            def shev(eng, i2t=i2t, tts=tuple(tts)):
                last = None
                for j, tt in enumerate(tts):
                    last = eng.activation(rf(O_SST + 2 * j * 512, 512)[:, 0:512] if False
                                          else rf(O_SST, 1536)[:, j * 512:(j + 1) * 512],
                                          pb[j][:, 0:512], AF.Silu,
                                          bias=bsv[:, 0 + i2t:1 + i2t])
                return last
            stage(f"sh{i2t}_s{r3}", scalar=shev)
            def shev3(eng, i2t=i2t, tts=tuple(tts)):
                last = None
                for j, tt in enumerate(tts):
                    t3 = rf(O_SST + 3072, 1536)[:, j * 512:(j + 1) * 512]
                    eng.scalar_tensor_tensor(t3, pb[3 + j][:, 0:512],
                                             bsv[:, 8 + i2t:9 + i2t],
                                             pb[3 + j][:, 0:512],
                                             ALU.bypass, ALU.bypass)
                    last = None
                return last
            def shev3b(eng, i2t=i2t, tts=tuple(tts)):
                last = None
                for j, tt in enumerate(tts):
                    last = eng.activation(rf(O_SST + 3072, 1536)[:, j * 512:(j + 1) * 512],
                                          pb[3 + j][:, 0:512], AF.Identity,
                                          bias=bsv[:, 8 + i2t:9 + i2t])
                return last
            stage(f"sh{i2t}_i{r3}", scalar=shev3b)
            def shmul(eng, i2t=i2t, tts=tuple(tts)):
                last = None
                for j, tt in enumerate(tts):
                    last = eng.tensor_tensor(rb(O_ZST + tt * 512, 512),
                                             rf(O_SST, 1536)[:, j * 512:(j + 1) * 512],
                                             rf(O_SST + 3072, 1536)[:, j * 512:(j + 1) * 512],
                                             ALU.mult)
                return last
            stage(f"sh{i2t}_v{r3}", vector=shmul)
        def shst(eng, i2t=i2t):
            dma(eng, "s", zsh_h.ap()[i2t * 128:(i2t + 1) * 128, :], rb(O_ZST, 4096))
            dwait(eng, "s")
        stage(f"sh{i2t}_st", sync=shst)

    O_CB = 0
    O_WE1 = 8300
    O_WE3 = 12400
    O_S1 = 16500
    O_HT = 24900
    O_HTB = 27000
    O_FSTA = 29100
    O_FSTB = 30200
    O_ET = 31300
    O_CBROW = 33400
    O_WE1B = 37500
    O_WE3B = 41600

    pbtf = [pbt[0].ap().bitcast(F32), pbt[1].ap().bitcast(F32)]

    def h3out(it):
        return pb[4 + it][:, 0:512] if it < 2 else pbtf[it - 2][:, 0:512]

    def wbase(e):
        return (O_WE1, O_WE3) if e % 2 == 0 else (O_WE1B, O_WE3B)

    def em_h1(eng, e, tt):
        w1, _ = wbase(e)
        last = None
        for it in range(4):
            for kd in range(8):
                last = eng.matmul(pb[it][:, 0:512],
                                  rb(w1 + kd * 512 + it * 128, 128),
                                  rb(O_STR + kd * 4096 + tt * 512, 512),
                                  start=(kd == 0), stop=(kd == 7))
        return last

    def em_h3(eng, e, tt):
        _, w3 = wbase(e)
        last = None
        for it in range(4):
            out = h3out(it)
            for kd in range(8):
                last = eng.matmul(out, rb(w3 + kd * 512 + it * 128, 128),
                                  rb(O_STR + kd * 4096 + tt * 512, 512),
                                  start=(kd == 0), stop=(kd == 7))
        return last

    def em_B1(eng, e, tt):
        last = None
        for it in range(4):
            last = eng.activation(rf(O_S1, 2048)[:, it * 512:(it + 1) * 512],
                                  pb[it][:, 0:512], AF.Silu,
                                  bias=bev[:, e * 4 + it:e * 4 + it + 1])
        return last

    def em_E(eng, e, tt):
        ht = O_HT if tt % 2 == 0 else O_HTB
        last = None
        for it in range(4):
            t = rf(O_ET, 512)
            eng.scalar_tensor_tensor(t, h3out(it),
                                     bev[:, 64 + e * 4 + it:65 + e * 4 + it],
                                     rf(O_S1, 2048)[:, it * 512:(it + 1) * 512],
                                     ALU.add, ALU.mult)
            last = eng.tensor_tensor(rb(ht + it * 512, 512), t,
                                     rf(O_CB, 4096)[:, tt * 512:(tt + 1) * 512],
                                     ALU.mult)
        return last

    def em_F(eng, e, tt):
        ht = O_HT if tt % 2 == 0 else O_HTB
        dst = AP(tensor=h_all_h.ap().tensor,
                 offset=e * INTER * LH + tt * 512,
                 ap=[[LH, 128], [128 * LH, 4], [1, 512]])
        dma(eng, "s", dst, rb(ht, 2048).rearrange("p (it k) -> p it k", it=4))

    def em_cbm(eng, e, half):
        last = None
        for i in range(4):
            tt = half * 4 + i
            out = pb[i][:, 0:512] if half == 0 else h3out(i)
            last = eng.matmul(out, onesb[0:1, :],
                              rb(O_CBROW + tt * 512, 512, p0=0, pn=1),
                              start=True, stop=True)
        return last

    def em_cbe(eng, half):
        last = None
        for i in range(4):
            tt = half * 4 + i
            src = pb[i][:, 0:512] if half == 0 else h3out(i)
            last = eng.tensor_copy(rf(O_CB, 4096)[:, tt * 512:(tt + 1) * 512], src)
        return last

    def em_wl(eng, e1, s):
        kd = s // 2
        stg = O_FSTA if s % 2 == 0 else O_FSTB
        src = inp["We1" if s % 2 == 0 else "We3"].ap()[e1, kd * 128:(kd + 1) * 128, :]
        dma(eng, "s", rf(stg, 512), src)

    def em_wc(eng, e1, s):
        kd = s // 2
        stg = O_FSTA if s % 2 == 0 else O_FSTB
        w1b, w3b = wbase(e1)
        dst = (w1b if s % 2 == 0 else w3b) + kd * 512
        return eng.tensor_copy(rb(dst, 512), rf(stg, 512))

    def fa_pre_cbd(eng):
        dma(eng, "s", rb(O_CBROW, 4096, p0=0, pn=1), comb_h.ap()[0:1, :])
        dwait(eng, "s")
    stage("fa_pre_cbd", sync=fa_pre_cbd)
    for kd in range(8):
        def wld0(eng, kd=kd):
            dma(eng, "s", rf(O_FSTA, 512), inp["We1"].ap()[0, kd * 128:(kd + 1) * 128, :])
            dma(eng, "s", rf(O_FSTB, 512), inp["We3"].ap()[0, kd * 128:(kd + 1) * 128, :])
            dwait(eng, "s")
        stage(f"fa0_wl{kd}", sync=wld0)
        def wcst0(eng, kd=kd):
            eng.tensor_copy(rb(O_WE1 + kd * 512, 512), rf(O_FSTA, 512))
            return eng.tensor_copy(rb(O_WE3 + kd * 512, 512), rf(O_FSTB, 512))
        stage(f"fa0_wc{kd}", vector=wcst0)

    for e in range(EXP):
        def t1t(eng, e=e):
            return em_cbm(eng, e, 0)
        def t1v(eng, e=e):
            last = None
            if e > 0:
                last = em_E(eng, e - 1, 7)
                last = em_wc(eng, e, 15)
            return last
        stage(f"fa{e}_T1", tensor=t1t, vector=t1v)

        def t2t(eng, e=e):
            return em_cbm(eng, e, 1)
        def t2v(eng, e=e):
            return em_cbe(eng, 0)
        def t2s(eng, e=e):
            if e > 0:
                em_F(eng, e - 1, 7)
                dwait(eng, "s")
        stage(f"fa{e}_T2", tensor=t2t, vector=t2v, sync=t2s)

        for s in range(16):
            tt = s // 2
            if s % 2 == 0:
                def ev(eng, e=e, s=s, tt=tt):
                    last = None
                    if s == 0:
                        last = em_cbe(eng, 1)
                    if tt > 0:
                        last = em_E(eng, e, tt - 1)
                    if e < EXP - 1 and s >= 2:
                        last = em_wc(eng, e + 1, s - 1)
                    return last
                def et(eng, e=e, tt=tt):
                    return em_h1(eng, e, tt)
                def es(eng, e=e, s=s):
                    if e < EXP - 1:
                        em_wl(eng, e + 1, s)
                        dwait(eng, "s")
                stage(f"fa{e}_s{s}", tensor=et, vector=ev, sync=es)
            else:
                def ov(eng, e=e, s=s):
                    if e < EXP - 1:
                        return em_wc(eng, e + 1, s - 1)
                def ot(eng, e=e, tt=tt):
                    return em_h3(eng, e, tt)
                def oc(eng, e=e, tt=tt):
                    return em_B1(eng, e, tt)
                def os_(eng, e=e, s=s, tt=tt):
                    did = False
                    if e < EXP - 1:
                        em_wl(eng, e + 1, s)
                        did = True
                    if s == 1 and e < EXP - 1:
                        dma(eng, "s", rb(O_CBROW, 4096, p0=0, pn=1),
                            comb_h.ap()[e + 1:e + 2, :])
                        did = True
                    if tt >= 1:
                        em_F(eng, e, tt - 1)
                        did = True
                    if did:
                        dwait(eng, "s")
                stage(f"fa{e}_s{s}", tensor=ot, scalar=oc, vector=ov, sync=os_)

    def fa_dr1(eng):
        return em_E(eng, EXP - 1, 7)
    stage("fa_dr1", vector=fa_dr1)
    def fa_dr2(eng):
        em_F(eng, EXP - 1, 7)
        dwait(eng, "s")
    stage("fa_dr2", sync=fa_dr2)

    O_W2STG = (0, 16600)
    O_W2CD = (33400, 37500)
    for e in range(EXP + 2):
        def w2sy(eng, e=e):
            did = False
            if e < EXP:
                for it in range(4):
                    dma(eng, "s", rf(O_W2STG[e % 2] + it * 4096, 1024),
                        inp["We2"].ap()[e, it * 128:(it + 1) * 128, :])
                did = True
            if e >= 2:
                e2 = e - 2
                for it in range(4):
                    dma(eng, "s",
                        we2b_h.ap()[e2 * 512 + it * 128: e2 * 512 + (it + 1) * 128, :],
                        rb(O_W2CD[e2 % 2] + it * 1024, 1024))
                did = True
            if did:
                dwait(eng, "s")
        def w2ve(eng, e=e):
            if 1 <= e <= EXP:
                e1 = e - 1
                last = None
                for it in range(4):
                    last = eng.tensor_copy(rb(O_W2CD[e1 % 2] + it * 1024, 1024),
                                           rf(O_W2STG[e1 % 2] + it * 4096, 1024))
                return last
        stage(f"w2_p{e}", sync=w2sy, vector=w2ve)

    x1_h = nc.dram_tensor("x1_h", [8, 128, LH], BF16, kind="Internal")
    def x1st(eng):
        dst = AP(tensor=x1_h.ap().tensor, offset=0,
                 ap=[[LH, 128], [128 * LH, 4], [1, LH]])
        dma(eng, "s", dst, rb(O_STR, 16384).rearrange("p (k l) -> p k l", k=4))
        dwait(eng, "s")
    def x1stg(eng):
        dst = AP(tensor=x1_h.ap().tensor, offset=4 * 128 * LH,
                 ap=[[LH, 128], [128 * LH, 4], [1, LH]])
        dma(eng, "g", dst, rb(O_STR + 16384, 16384).rearrange("p (k l) -> p k l", k=4))
        dwait(eng, "g")
    stage("x1_st", sync=x1st, gpsimd=x1stg)

    O_W2H = 0
    O_HB2 = (32800, 65700)
    O_WS2H = 49200
    O_ZS2 = (53400, 55450)
    O_X1T2 = (57500, 59550)
    O_EVS2 = (61600, 63650)
    O_WSTG = 61900

    def fb_bank(p, dt):
        if p == 0:
            return pb[dt][:, 0:256]
        return pb[4 + dt][:, 0:256] if dt < 2 else pbtf[dt - 2][:, 0:256]

    def em_fbl(eng, tt, q, half):
        p = tt % 2
        src = AP(tensor=h_all_h.ap().tensor,
                 offset=tt * 256 + half * 8 * INTER * LH,
                 ap=[[LH, 128], [INTER * LH, 8], [128 * LH, 4], [1, 256]])
        dma(eng, q, rb(O_HB2[p] + half * 8192, 8192).rearrange(
            "p (e it k) -> p e it k", e=8, it=4), src)
        if half == 0:
            src2 = AP(tensor=zsh_h.ap().tensor, offset=tt * 256,
                      ap=[[LH, 128], [128 * LH, 8], [1, 256]])
            dma(eng, q, rb(O_ZS2[p], 2048).rearrange("p (i k) -> p i k", k=256), src2)

    def em_fblx(eng, tt):
        p = tt % 2
        src3 = AP(tensor=x1_h.ap().tensor, offset=tt * 256,
                  ap=[[LH, 128], [128 * LH, 8], [1, 256]])
        dma(eng, "s", rb(O_X1T2[p], 2048).rearrange("p (i k) -> p i k", k=256), src3)

    def em_fbm(eng, dh, tt):
        p = tt % 2
        last = None
        for dt in range(4):
            out = fb_bank(p, dt)
            for kt in range(64):
                last = eng.matmul(out,
                                  rb(O_W2H + kt * 512 + dt * 128, 128),
                                  rb(O_HB2[p] + kt * 256, 256),
                                  start=(kt == 0), stop=False)
            for i2t in range(8):
                last = eng.matmul(out,
                                  rb(O_WS2H + i2t * 512 + dt * 128, 128),
                                  rb(O_ZS2[p] + i2t * 256, 256),
                                  start=False, stop=False)
            last = eng.matmul(out,
                              be2b[0:16, dh * 512 + dt * 128: dh * 512 + (dt + 1) * 128],
                              rb(82100, 4096, p0=0, pn=16)[:, tt * 256:(tt + 1) * 256],
                              start=False, stop=True)
        return last

    def em_fber(eng, dh, tt):
        p = tt % 2
        last = None
        for dt in range(4):
            kd = dh * 4 + dt
            ev = rf(O_EVS2[p], 1024)[:, dt * 256:(dt + 1) * 256]
            last = eng.scalar_tensor_tensor(ev, fb_bank(p, dt),
                                            bsv[:, 16 + dh * 4 + dt:17 + dh * 4 + dt],
                                            rb(O_X1T2[p] + kd * 256, 256),
                                            ALU.add, ALU.add)
        return last

    def em_fbst(eng, dh, tt):
        p = tt % 2
        for dt in range(4):
            dma(eng, "s",
                preln2_h.ap()[(dh * 4 + dt) * 128:(dh * 4 + dt + 1) * 128,
                              tt * 256:(tt + 1) * 256],
                rf(O_EVS2[p], 1024)[:, dt * 256:(dt + 1) * 256])

    for dh in range(2):
        def w2hl(eng, dh=dh):
            src = AP(tensor=we2b_h.ap().tensor, offset=dh * 512,
                     ap=[[1024, 128], [128 * 1024, 32], [1, 512]])
            dma(eng, "s", rb(O_W2H, 16384).rearrange("p (kt c) -> p kt c", c=512), src)
            dwait(eng, "s")
        def w2hlg(eng, dh=dh):
            src = AP(tensor=we2b_h.ap().tensor, offset=dh * 512 + 32 * 128 * 1024,
                     ap=[[1024, 128], [128 * 1024, 32], [1, 512]])
            dma(eng, "g", rb(O_W2H + 16384, 16384).rearrange(
                "p (kt c) -> p kt c", c=512), src)
            dwait(eng, "g")
        stage(f"fb{dh}_wl", sync=w2hl, gpsimd=w2hlg)
        def ws2l(eng, dh=dh):
            for i2t in range(8):
                dma(eng, "s", rf(O_WSTG + i2t * 1024, 512),
                    inp["ws2"].ap()[i2t * 128:(i2t + 1) * 128, dh * 512:(dh + 1) * 512])
            dwait(eng, "s")
        stage(f"fb{dh}_w2l", sync=ws2l)
        def ws2c(eng, dh=dh):
            last = None
            for i2t in range(8):
                last = eng.tensor_copy(rb(O_WS2H + i2t * 512, 512),
                                       rf(O_WSTG + i2t * 1024, 512))
            return last
        stage(f"fb{dh}_w2c", vector=ws2c)

        for s in range(19):
            def fsy(eng, dh=dh, s=s):
                did = False
                if s < 16:
                    em_fbl(eng, s, "s", 0); did = True
                if 1 <= s <= 16:
                    em_fblx(eng, s - 1); did = True
                if s >= 3:
                    em_fbst(eng, dh, s - 3); did = True
                if did:
                    dwait(eng, "s")
            def fgp(eng, dh=dh, s=s):
                if s < 16:
                    em_fbl(eng, s, "g", 1)
                    dwait(eng, "g")
            def ften(eng, dh=dh, s=s):
                if 1 <= s <= 16:
                    return em_fbm(eng, dh, s - 1)
            def fvec(eng, dh=dh, s=s):
                if 2 <= s <= 17:
                    return em_fber(eng, dh, s - 2)
            stage(f"fb{dh}_p{s}", sync=fsy, gpsimd=fgp, tensor=ften, vector=fvec)

    for tt in range(8):
        def l2l(eng, tt=tt):
            for kd in range(8):
                dma(eng, "s", rf(0 + kd * 1024, 512),
                    preln2_h.ap()[kd * 128:(kd + 1) * 128, tt * 512:(tt + 1) * 512])
            dwait(eng, "s")
        stage(f"l2{tt}_l", sync=l2l)
        def l2sq(eng, tt=tt):
            last = None
            for kd in range(8):
                last = eng.activation(rb(16400 + kd * 512, 512), rf(kd * 1024, 512),
                                      AF.Square)
            return last
        stage(f"l2{tt}_sq", scalar=l2sq)
        def l2cs(eng, tt=tt):
            last = None
            for kd in range(8):
                last = eng.tensor_copy(rb(20500 + kd * 512, 512), rf(kd * 1024, 512))
            return last
        stage(f"l2{tt}_cs", vector=l2cs)
        def l2mm(eng, tt=tt):
            last = None
            for kd in range(8):
                last = eng.matmul(pb[4][0:1, 0:512], onesb[:, 0:1],
                                  rb(20500 + kd * 512, 512),
                                  start=(kd == 0), stop=(kd == 7))
            for kd in range(8):
                last = eng.matmul(pb[5][0:1, 0:512], onesb[:, 0:1],
                                  rb(16400 + kd * 512, 512),
                                  start=(kd == 0), stop=(kd == 7))
            return last
        stage(f"l2{tt}_mm", tensor=l2mm)
        def l2st(eng, tt=tt):
            m = tmp1[0:1, 0:512]
            v = tmp2[0:1, 0:512]
            eng.tensor_scalar_mul(m, pb[4][0:1, 0:512], 1.0 / 1024.0)
            eng.tensor_scalar_mul(v, pb[5][0:1, 0:512], 1.0 / 1024.0)
            t = tmp3[0:1, 0:512]
            eng.tensor_mul(t, m, m)
            eng.tensor_sub(v, v, t)
            return eng.tensor_copy(mrb[0:1, 0:512], m)
        stage(f"l2{tt}_st", vector=l2st)
        def l2rs(eng, tt=tt):
            return eng.activation(tmp2[0:1, 0:512], tmp2[0:1, 0:512], AF.Sqrt,
                                  bias=epsv[0:1, 0:1])
        stage(f"l2{tt}_rs", scalar=l2rs)
        def l2rc(eng, tt=tt):
            eng.reciprocal(tmp2[0:1, 0:512], tmp2[0:1, 0:512])
            return eng.tensor_copy(mrb[0:1, 512:1024], tmp2[0:1, 0:512])
        stage(f"l2{tt}_rc", vector=l2rc)
        def l2bm(eng, tt=tt):
            eng.matmul(pb[4][:, 0:512], onesb[0:1, :], mrb[0:1, 0:512],
                       start=True, stop=True)
            return eng.matmul(pb[5][:, 0:512], onesb[0:1, :], mrb[0:1, 512:1024],
                              start=True, stop=True)
        stage(f"l2{tt}_bm", tensor=l2bm)
        def l2ap(eng, tt=tt):
            last = None
            for kd in range(8):
                x2 = rf(kd * 1024, 512)
                eng.tensor_tensor(x2, x2, pb[4][:, 0:512], ALU.subtract)
                last = eng.tensor_tensor(x2, x2, pb[5][:, 0:512], ALU.mult)
            return last
        stage(f"l2{tt}_ap", vector=l2ap)
        for sub in range(4):
            def l2tr(eng, tt=tt, sub=sub):
                last = None
                for kd in range(8):
                    last = eng.transpose(
                        pb[kd // 4][:, (kd % 4) * 128:(kd % 4) * 128 + 128],
                        rf(kd * 1024, 512)[:, sub * 128:(sub + 1) * 128], idn[:])
                return last
            stage(f"l2{tt}s{sub}_t", tensor=l2tr)
            def l2te(eng, tt=tt, sub=sub):
                last = None
                for kd in range(8):
                    last = eng.tensor_copy(
                        rf(24700 + sub * 2048 + kd * 256, 128),
                        pb[kd // 4][:, (kd % 4) * 128:(kd % 4) * 128 + 128])
                return last
            stage(f"l2{tt}s{sub}_e", vector=l2te)
        def l2out(eng, tt=tt):
            for sub in range(4):
                dma(eng, "s",
                    out_t.ap()[tt * 512 + sub * 128: tt * 512 + (sub + 1) * 128, :],
                    rf(24700 + sub * 2048, 1024))
            dwait(eng, "s")
        stage(f"l2{tt}_o", sync=l2out)

    taps = {
        "u": ([H, LH], BF16, u_h),
        "zs": ([H, LH], BF16, zs_h),
        "kpt": ([64, H], BF16, kpT_h),
        "ipv": ([H, 2, QQ, N8], BF16, ipv_h),
        "opv": ([H, 2, N8, QQ], BF16, opv_h),
        "mu": ([2, N8, H], F32, muT_h),
        "y": ([H, LH], BF16, y_h),
        "uslab": ([4, 128, 128 * 128], BF16, u_slab_h),
        "x1": ([8, 128, LH], BF16, x1_h),
        "zsh": ([DM, LH], BF16, zsh_h),
        "hall": ([EXP, INTER, LH], BF16, h_all_h),
        "preln2": ([DM, LH], F32, preln2_h),
        "we2b": ([EXP * INTER, DM], BF16, we2b_h),
        "comb2": ([EXP, LH], BF16, comb_h),
    }
    for name in dbg_names:
        shape, dt, src_t = taps[name]
        dbg_out[name] = nc.dram_tensor("dbg_" + name, shape, dt, kind="ExternalOutput")
        def cp(eng, d=dbg_out[name], s=src_t):
            dma(eng, "s", d.ap(), s.ap())
            dwait(eng, "s")
        stage("dbg_" + name, sync=cp)

    for (anm, after, aoff, ancols, ap0, apn, adt) in dbg_arena:
        dbg_out[anm] = nc.dram_tensor("dbg_" + anm, [apn, ancols], adt,
                                      kind="ExternalOutput")
        def acp(eng, d=dbg_out[anm], aoff=aoff, ancols=ancols, ap0=ap0, apn=apn, adt=adt):
            srcap = (arena[ap0:ap0 + apn, aoff:aoff + ancols] if adt == BF16
                     else arena[ap0:ap0 + apn, aoff:aoff + 2 * ancols].bitcast(F32))
            dma(eng, "s", d.ap(), srcap)
            dwait(eng, "s")
        idxs = [i for i, (nm, _) in enumerate(stages) if nm == after]
        stages.insert(idxs[0] + 1, ("atap_" + anm, {"sync": acp}))

    ktr = int(os.environ.get("KTRUNC", "0"))
    if ktr > 0:
        del stages[ktr:]
    if os.environ.get("KSTAGES"):
        for i, (nm, _) in enumerate(stages):
            print(i, nm)
    with nc.Block() as block:
        def run(name):
            def go(eng):
                for k, (sname, fns) in enumerate(stages):
                    eng.wait_ge(bar, 5 * k)
                    last = None
                    if name in fns:
                        last = fns[name](eng)
                    if last is not None and hasattr(last, "then_inc"):
                        last.then_inc(bar, 1)
                    else:
                        eng.sem_inc(bar, 1)
            return go
        block.sync(run("sync"))
        block.vector(run("vector"))
        block.tensor(run("tensor"))
        block.scalar(run("scalar"))
        block.gpsimd(run("gpsimd"))

    st.close()
    return nc, dbg_out


_CACHE = {}

def _in_maps(inputs):
    x = np.ascontiguousarray(inputs["L0_attn_hidden_states"])
    B, L, Dm = x.shape
    Lh = L // 2
    wmap = {
        "W_in": "L0_attn_W_in", "conv_w": "L0_attn_conv_w", "conv_b": "L0_attn_conv_b",
        "log_dt": "L0_attn_log_dt", "log_A_real": "L0_attn_log_A_real",
        "A_imag": "L0_attn_A_imag", "C_re": "L0_attn_C_re", "C_im": "L0_attn_C_im",
        "D": "L0_attn_D", "W_out": "L0_attn_W_out",
        "gate_w": "L0_ffn_gate_w", "gate_b": "L0_ffn_gate_b",
        "We1": "L0_ffn_We1", "be1": "L0_ffn_be1", "We2": "L0_ffn_We2",
        "be2": "L0_ffn_be2", "We3": "L0_ffn_We3", "be3": "L0_ffn_be3",
        "ws1": "L0_ffn_ws1", "bs1": "L0_ffn_bs1", "ws2": "L0_ffn_ws2",
        "bs2": "L0_ffn_bs2", "ws3": "L0_ffn_ws3", "bs3": "L0_ffn_bs3",
    }
    base = {k: np.ascontiguousarray(inputs[v]) for k, v in wmap.items()}
    maps = []
    for c in range(NCORE):
        b, half = c // 2, c % 2
        xs = x[b, half * Lh:(half + 1) * Lh]
        halo = (np.zeros((3, Dm), np.float32) if half == 0
                else x[b, half * Lh - 3:half * Lh])
        selv = np.zeros((1, NCORE), np.float32)
        if half == 1:
            selv[0, c - 1] = 1.0
        m = dict(base)
        m["x"] = np.ascontiguousarray(np.concatenate([halo, xs], 0))
        m["sel"] = selv
        maps.append(m)
    return maps, (B, L, Dm)


def kernel(**inputs):
    if "nc" not in _CACHE:
        _CACHE["nc"] = build_nc()
    nc, _ = _CACHE["nc"]
    maps, (B, L, Dm) = _in_maps(inputs)
    res = run_bass_kernel_spmd(nc, maps, core_ids=list(range(NCORE)))
    outs = [res.results[c]["out"] for c in range(NCORE)]
    return np.stack(outs, 0).reshape(B, L, Dm)
